# version 01
# baseline (speedup 1.0000x reference)
import contextlib
import numpy as np
import ml_dtypes
import concourse.bass as bass
import concourse.mybir as mybir
from concourse.bass_utils import run_bass_kernel_spmd

F32 = mybir.dt.float32
BF16 = mybir.dt.bfloat16
AF = mybir.ActivationFunctionType
ALU = mybir.AluOpType
AX = mybir.AxisListType.X

N_CORE = 8
T = 2048
D = 1024
H = 16
NT = 4
NESF = 4096
OF = 2112

ENGINES = ("tensor", "vector", "scalar", "sync", "gpsimd")


class Sched:
    def __init__(self):
        self.q = {e: [] for e in ENGINES}
        self.sems = {}
        self.cnt = {}
        self.lastwait = {e: {} for e in ENGINES}

    def add_sem(self, name, handle):
        self.sems[name] = handle
        self.cnt[name] = 0

    OWN = {"vector": "sV", "scalar": "sS"}

    def emit(self, eng, fn=None, waits=(), inc=None, inc_amt=1):
        own = self.OWN.get(eng)
        if own is not None and fn is not None:
            waits = list(waits) + [(own, self.cnt[own])]
            if inc is None:
                inc = own
        wl = []
        for s, v in waits:
            if not v or v <= 0:
                continue
            if s in ("sW", "sIO"):
                v = self.cnt[s]
            if v > self.lastwait[eng].get(s, -1):
                wl.append((self.sems[s], v))
                self.lastwait[eng][s] = v
        post = None
        if inc is not None:
            if fn is None:
                raise RuntimeError("inc without instruction")
            self.cnt[inc] += inc_amt
            post = (self.sems[inc], inc_amt)
        self.q[eng].append((tuple(wl), fn, post))
        return self.cnt[inc] if inc is not None else None

    def marks(self):
        return dict(self.cnt)

    def flush(self, block):
        for eng in ENGINES:
            items, self.q[eng] = self.q[eng], []
            if not items:
                continue

            def runner(engobj, items=items):
                for wl, fn, post in items:
                    for s, v in wl:
                        engobj.wait_ge(s, v)
                    if fn is not None:
                        inst = fn(engobj)
                        if post is not None:
                            inst.then_inc(post[0], post[1])

            getattr(block, eng)(runner)


def _host_consts():
    bf = ml_dtypes.bfloat16
    ident = np.eye(128, dtype=bf)
    mneg = np.where(np.arange(128)[None, :] > np.arange(128)[:, None],
                    np.float32(-1e30), np.float32(0.0)).astype(np.float32)
    tie = np.linspace(0.0, 1e-6, 32, dtype=np.float32)
    tie_tile = np.tile(tie, (128, 16)).astype(np.float32)
    E = np.zeros((128, 32 * 128), dtype=bf)
    for j in range(32):
        for q in range(4):
            E[4 * j + q, 128 * j + 32 * q: 128 * j + 32 * q + 32] = 1
    return ident, mneg, tie_tile, E


RACE_CHECK = [False]


def build_program(stop_after=None, p1_tiles=NT, cut="full", p2_heads=H):
    nc = bass.Bass("TRN2", target_bir_lowering=False,
                   detect_race_conditions=RACE_CHECK[0])

    x_in = nc.dram_tensor("x", [T, D], F32, kind="ExternalInput").ap()
    wkqv = nc.dram_tensor("wkqv", [H, D, OF], F32, kind="ExternalInput").ap()
    bkqv_c = nc.dram_tensor("bkqv_c", [128, H * 17], F32, kind="ExternalInput").ap()
    wp = nc.dram_tensor("wp", [D, D], F32, kind="ExternalInput").ap()
    bp_c = nc.dram_tensor("bp_c", [128, 8], F32, kind="ExternalInput").ap()
    ctrl = nc.dram_tensor("ctrl", [D, 128], F32, kind="ExternalInput").ap()
    f1 = nc.dram_tensor("f1", [D, NESF], F32, kind="ExternalInput").ap()
    fbias_dc = nc.dram_tensor("fbias_c", [128, 32], F32, kind="ExternalInput").ap()
    f2 = nc.dram_tensor("f2", [NESF, D], F32, kind="ExternalInput").ap()
    c_ident = nc.dram_tensor("c_ident", [128, 128], BF16, kind="ExternalInput").ap()
    c_mneg = nc.dram_tensor("c_mneg", [128, 128], F32, kind="ExternalInput").ap()
    c_tie = nc.dram_tensor("c_tie", [128, 512], F32, kind="ExternalInput").ap()
    c_E = nc.dram_tensor("c_E", [128, 32 * 128], BF16, kind="ExternalInput").ap()
    out = nc.dram_tensor("out", [T, D], F32, kind="ExternalOutput").ap()
    xs1 = nc.dram_tensor("xs1", [T, D], F32, kind="Internal").ap()
    xs2 = nc.dram_tensor("xs2", [T, D], F32, kind="Internal").ap()

    def x4(t):
        return t.rearrange("(T s p) d -> T p s d", s=4, p=128)

    sch = Sched()
    stk = contextlib.ExitStack()
    with stk:
        for nm in ("sW", "sIO", "sT", "sV", "sS"):
            sch.add_sem(nm, stk.enter_context(nc.semaphore(nm)))

        ident = stk.enter_context(nc.sbuf_tensor("ident", [128, 128], BF16))
        mneg = stk.enter_context(nc.sbuf_tensor("mneg", [128, 128], F32))
        bkqv_sb = stk.enter_context(nc.sbuf_tensor("bkqv_sb", [128, H * 17], F32))
        bkqvq_sb = stk.enter_context(nc.sbuf_tensor("bkqvq_sb", [128, H * 17], F32))
        bp_sb = stk.enter_context(nc.sbuf_tensor("bp_sb", [128, 8], F32))
        sqscr = stk.enter_context(nc.sbuf_tensor("sqscr", [128, 2048], F32))
        stat = stk.enter_context(nc.sbuf_tensor("stat", [128, 32], F32))
        smx = stk.enter_context(nc.sbuf_tensor("smx", [128, 16], F32))
        psF = stk.enter_context(nc.psum_tensor("psF", [128, 3072], F32))
        psT = stk.enter_context(nc.psum_tensor("psT", [128, 2048], BF16))

        blk = stk.enter_context(nc.Block())

        def dma(queue, dst, src, waits=()):
            sem = "sW" if queue == "gpsimd" else "sIO"
            waits = list(waits) + [(sem, sch.cnt[sem])]
            return sch.emit(queue, lambda e, d=dst, s=src: e.dma_start(d, s),
                            waits=waits, inc=sem, inc_amt=16)

        def bank(b, w=512, off=0):
            return psF[:, 512 * b + off: 512 * b + off + w]

        dma("sync", ident[:], c_ident)
        dma("sync", mneg[:], c_mneg)
        dma("sync", bkqv_sb[:], bkqv_c)
        dma("sync", bp_sb[:], bp_c)
        c_loaded = sch.cnt["sIO"]
        sch.emit("vector", lambda e: e.tensor_scalar_mul(
            bkqvq_sb[:], bkqv_sb[:], 1.0 / 32.0),
            waits=[("sIO", c_loaded)], inc="sV")
        sch.emit("vector", lambda e: e.memset(smx[:], 1.0), inc="sV")
        v_init = sch.cnt["sV"]

        psTg = {"n": 0, "evac": {}}

        def psT_group(tiles, evac_fn, waits_extra=()):
            g = psTg["n"]
            base = (g % 2) * 1024
            w = list(waits_extra) + [("sV", psTg["evac"].get(g - 2, 0))]
            n = len(tiles)
            for q, tf in enumerate(tiles):
                sch.emit("tensor",
                         lambda e, tf=tf, col=base + 128 * q: tf(e, col),
                         waits=w if q == 0 else [],
                         inc="sT" if q == n - 1 else None)
            td = sch.cnt["sT"]
            sch.emit("vector", lambda e, col=base: evac_fn(e, col),
                     waits=[("sT", td)], inc="sV")
            psTg["evac"][g] = sch.cnt["sV"]
            psTg["n"] += 1
            return td, sch.cnt["sV"]

        def ffn_phase(xsrc, spill_dst, ln2dst, outdst, buf, entry, ntiles=NT, cut="full"):
            (f1_sb, mh_sb, xtile, lnfm, updf, f2sl, ctrl_sb, fbias_sb,
             tie_sb, E_sb, logit_sb, perm_sb, hstage, ln32c) = buf
            lo_sb = mh_sb
            eT, eV, eS, eIO = (entry.get("sT", 0), entry.get("sV", 0),
                              entry.get("sS", 0), entry.get("sIO", 0))

            for c in range(8):
                dma("gpsimd", f1_sb[:, 4096 * c: 4096 * (c + 1)],
                    f1.rearrange("(c p) n -> c p n", p=128)[c],
                    waits=([("sT", eT), ("sV", eV), ("sS", eS), ("sIO", eIO)]
                           if c == 0 else ()))
            f1_done = sch.cnt["sW"]
            dma("gpsimd", fbias_sb[:], fbias_dc)
            w_base = sch.cnt["sW"]
            dma("sync", ctrl_sb[:].rearrange("p (c n) -> p c n", c=8),
                ctrl.rearrange("(c p) n -> p c n", p=128),
                waits=[("sT", eT), ("sV", eV), ("sS", eS)])
            dma("sync", tie_sb[:], c_tie,
                waits=[("sT", eT), ("sV", eV), ("sS", eS)])
            dma("sync", E_sb[:], c_E)
            io_consts = sch.cnt["sIO"]

            f2_dma = {}
            f2_used = {}
            stt_mark = {}
            perm_done_prev = 0
            last_hphase_mm = 0
            last_f2_mm = 0
            f2evac = eS
            updf_guard = eT
            x_cons_s = eS
            x_cons_v = eV
            x_cons_t = eT

            for Ti in range(ntiles):
                dma("sync", xtile[:].rearrange("p (s d) -> p s d", s=4),
                    x4(xsrc)[Ti],
                    waits=[("sS", x_cons_s), ("sV", x_cons_v), ("sT", x_cons_t)])
                x_here = sch.cnt["sIO"]
                x_cons_t = 0

                def emit_f2_dma(useq, Ti=Ti):
                    gu = 16 * Ti + useq
                    u = useq % 8
                    dh = useq // 8
                    s_ap = f2.rearrange("(u c p) d -> u p c d", c=4, p=128)[u]
                    s_ap = s_ap[:, :, 512 * dh: 512 * dh + 512]
                    dma("gpsimd",
                        f2sl[gu % 3][:].rearrange("p (c d) -> p c d", c=4),
                        s_ap, waits=[("sT", f2_used.get(gu - 3, 0))])
                    f2_dma[gu] = sch.cnt["sW"]

                for useq in range(3):
                    emit_f2_dma(useq)

                for s in range(4):
                    sub = xtile[:, 1024 * s: 1024 * (s + 1)]
                    sch.emit("vector", lambda e, sub=sub, s=s: e.reduce_sum(
                        stat[:, s:s + 1], sub, AX),
                        waits=[("sIO", x_here), ("sV", v_init)], inc="sV")
                    sch.emit("scalar", lambda e, sub=sub, s=s: e.activation(
                        sqscr[:, 0:1024], sub, AF.Square,
                        accum_out=stat[:, 4 + s:5 + s]),
                        waits=[("sIO", x_here), ("sS", sch.cnt["sS"])], inc="sS")
                sq_done = sch.cnt["sS"]
                sch.emit("vector", lambda e: e.tensor_scalar_mul(
                    stat[:, 8:12], stat[:, 0:4], 1.0 / 1024.0),
                    waits=[("sV", sch.cnt["sV"])], inc="sV")
                sch.emit("vector", lambda e: e.tensor_scalar_mul(
                    stat[:, 12:16], stat[:, 4:8], 1.0 / 1024.0),
                    waits=[("sS", sq_done), ("sV", sch.cnt["sV"])], inc="sV")
                sch.emit("vector", lambda e: e.scalar_tensor_tensor(
                    stat[:, 16:20], stat[:, 8:12], -1.0, stat[:, 8:12],
                    ALU.mult, ALU.mult),
                    waits=[("sV", sch.cnt["sV"])], inc="sV")
                sch.emit("vector", lambda e: e.tensor_add(
                    stat[:, 12:16], stat[:, 12:16], stat[:, 16:20]),
                    waits=[("sV", sch.cnt["sV"])], inc="sV")
                sch.emit("vector", lambda e: e.tensor_scalar_add(
                    stat[:, 12:16], stat[:, 12:16], 1e-5),
                    waits=[("sV", sch.cnt["sV"])], inc="sV")
                var_done = sch.cnt["sV"]
                sch.emit("scalar", lambda e: e.activation(
                    stat[:, 16:20], stat[:, 12:16], AF.Ln),
                    waits=[("sV", var_done), ("sS", sch.cnt["sS"])], inc="sS")
                sch.emit("scalar", lambda e: e.activation(
                    stat[:, 20:24], stat[:, 16:20], AF.Exp, scale=-0.5),
                    waits=[("sS", sch.cnt["sS"])], inc="sS")
                inv_done = sch.cnt["sS"]
                sch.emit("vector", lambda e: e.scalar_tensor_tensor(
                    stat[:, 24:28], stat[:, 8:12], -1.0, stat[:, 20:24],
                    ALU.mult, ALU.mult),
                    waits=[("sS", inv_done), ("sV", sch.cnt["sV"])], inc="sV")
                nmi = sch.cnt["sV"]
                hilo_v = {}
                for s in range(4):
                    half = (s % 2) * 1024
                    wsx = [("sV", nmi), ("sT", updf_guard),
                           ("sV", hilo_v.get(s - 2, 0))]
                    if Ti > 0 and s == 0:
                        wsx.append(("sV", perm_done_prev))
                    sch.emit("scalar", lambda e, s=s, half=half: e.activation(
                        sqscr[:, half: half + 1024],
                        xtile[:, 1024 * s: 1024 * (s + 1)], AF.Identity,
                        bias=stat[:, 24 + s: 25 + s], scale=stat[:, 20 + s: 21 + s]),
                        waits=wsx + [("sS", sch.cnt["sS"])], inc="sS")
                    ap_s = sch.cnt["sS"]
                    sch.emit("vector", lambda e, s=s, half=half: e.tensor_copy(
                        updf[:, 1024 * s: 1024 * (s + 1)],
                        sqscr[:, half: half + 1024]),
                        waits=[("sS", ap_s), ("sT", last_f2_mm)], inc="sV")
                    sch.emit("vector", lambda e, s=s, half=half: e.tensor_tensor(
                        lo_sb[:, 1024 * s: 1024 * (s + 1)],
                        sqscr[:, half: half + 1024],
                        updf[:, 1024 * s: 1024 * (s + 1)], ALU.subtract),
                        inc="sV")
                    hilo_v[s] = sch.cnt["sV"]
                ln_app_v = sch.cnt["sV"]
                ln_app = sch.cnt["sS"]
                if cut == "ln":
                    continue

                lgmm = 0
                for c in range(8):
                    hi_tiles = [
                        (lambda e, col, c=c, s=s: e.transpose(
                            psT[:, col: col + 128],
                            updf[:, 1024 * s + 128 * c: 1024 * s + 128 * (c + 1)],
                            ident[:]))
                        for s in range(4)]
                    wx = [("sV", ln_app_v)]
                    if Ti == 0 and c == 0:
                        wx.append(("sT", eT))
                    psT_group(hi_tiles,
                              lambda e, col, c=c: e.tensor_copy(
                                  lnfm[:, 512 * c: 512 * (c + 1)],
                                  psT[:, col: col + 512]),
                              waits_extra=wx + [("sT", last_hphase_mm)])
                    lo_tiles = [
                        (lambda e, col, c=c, s=s: e.transpose(
                            psT[:, col: col + 128],
                            lo_sb[:, 1024 * s + 128 * c: 1024 * s + 128 * (c + 1)],
                            ident[:]))
                        for s in range(4)]
                    psT_group(lo_tiles,
                              lambda e, col, c=c: e.tensor_tensor(
                                  ln32c[:], lnfm[:, 512 * c: 512 * (c + 1)],
                                  psT[:, col: col + 512], ALU.add),
                              waits_extra=[("sT", lgmm)])
                    ln32_done = sch.cnt["sV"]
                    w = [("sV", ln32_done)]
                    if c == 0:
                        w += [("sIO", io_consts), ("sS", f2evac)]
                    sch.emit("tensor", lambda e, c=c: e.matmul(
                        bank(0), ctrl_sb[:, 128 * c: 128 * (c + 1)], ln32c[:],
                        start=(c == 0), stop=(c == 7)),
                        waits=w, inc="sT")
                    lgmm = sch.cnt["sT"]
                lnfm_done = sch.cnt["sV"]
                lg_done = lgmm
                if cut == "lnT":
                    continue
                sch.emit("vector", lambda e: e.tensor_tensor(
                    logit_sb[:], bank(0), tie_sb[:], ALU.add),
                    waits=[("sT", lg_done), ("sIO", io_consts)], inc="sV")
                sch.emit("vector", lambda e: e.reduce_max(
                    sqscr[:, 0:16].rearrange("p (g o) -> p g o", o=1),
                    logit_sb[:].rearrange("p (g t) -> p g t", g=16), AX),
                    waits=[("sS", sq_done)], inc="sV")
                gmax_mark = sch.cnt["sV"]
                for g in range(16):
                    w = [("sT", last_hphase_mm)] if g == 0 else []
                    if g == 0:
                        w.append(("sV", gmax_mark))
                    sch.emit("vector", lambda e, g=g: e.tensor_single_scalar(
                        perm_sb[:, 32 * g: 32 * (g + 1)],
                        logit_sb[:, 32 * g: 32 * (g + 1)],
                        sqscr[:, g: g + 1], ALU.is_equal),
                        waits=w, inc="sV" if g == 15 else None)
                perm_done = sch.cnt["sV"]
                perm_done_prev = perm_done
                if cut == "logits":
                    continue

                for j in range(32):
                    gk = 32 * Ti + j
                    mb = 1 + (j % 2)
                    hb = 3 + (j % 2)
                    wm = [("sV", perm_done), ("sV", stt_mark.get(gk - 2, 0))]
                    if j < 2:
                        wm.append(("sS", f2evac))
                    sch.emit("tensor", lambda e, j=j, mb=mb: e.matmul(
                        bank(mb), E_sb[:, 128 * j: 128 * (j + 1)], perm_sb[:],
                        start=True, stop=True), waits=wm, inc="sT")
                    for c in range(8):
                        w = [("sW", f1_done)] if (Ti == 0 and j == 0 and c == 0) else []
                        sch.emit("tensor", lambda e, j=j, c=c, hb=hb: e.matmul(
                            bank(hb),
                            f1_sb[:, 4096 * c + 128 * j: 4096 * c + 128 * (j + 1)],
                            lnfm[:, 512 * c: 512 * (c + 1)],
                            start=(c == 0), stop=(c == 7)),
                            waits=w, inc="sT" if c == 7 else None)
                    hj = sch.cnt["sT"]
                    if j == 31:
                        last_hphase_mm = hj
                    hs = (j % 2) * 512
                    sch.emit("scalar", lambda e, j=j, hb=hb, hs=hs: e.activation(
                        hstage[:, hs: hs + 512], bank(hb), AF.Relu,
                        bias=fbias_sb[:, j: j + 1]),
                        waits=[("sT", hj), ("sV", stt_mark.get(gk - 2, 0)),
                               ("sW", w_base)], inc="sS")
                    rl = sch.cnt["sS"]
                    sch.emit("vector", lambda e, j=j, mb=mb, hs=hs: e.tensor_tensor(
                        mh_sb[:, 512 * j: 512 * (j + 1)],
                        hstage[:, hs: hs + 512], bank(mb), ALU.mult),
                        waits=[("sS", rl), ("sT", last_f2_mm)], inc="sV")
                    stt_mark[gk] = sch.cnt["sV"]
                mh_done = stt_mark[32 * Ti + 31]
                if cut == "h":
                    continue

                for dh in range(2):
                    for k in range(32):
                        useq = 8 * dh + (k // 4)
                        gu = 16 * Ti + useq
                        for dbl in range(4):
                            w = []
                            if k == 0 and dbl == 0:
                                w = [("sV", mh_done), ("sS", f2evac)]
                            if dbl == 0 and k % 4 == 0:
                                w.append(("sW", f2_dma[gu]))
                            last = (k % 4 == 3 and dbl == 3)
                            sch.emit("tensor", lambda e, k=k, dbl=dbl, gu=gu: e.matmul(
                                bank(dbl),
                                f2sl[gu % 3][:, 512 * (k % 4) + 128 * dbl:
                                             512 * (k % 4) + 128 * (dbl + 1)],
                                mh_sb[:, 512 * k: 512 * (k + 1)],
                                start=(k == 0), stop=(k == 31)),
                                waits=w, inc="sT" if last else None)
                            if last:
                                f2_used[gu] = sch.cnt["sT"]
                                if useq + 3 < 16:
                                    emit_f2_dma(useq + 3)
                    last_f2_mm = sch.cnt["sT"]
                    for dbl in range(4):
                        sch.emit("scalar", lambda e, dh=dh, dbl=dbl: e.activation(
                            updf[:, 512 * (4 * dh + dbl): 512 * (4 * dh + dbl) + 512],
                            bank(dbl), AF.Copy, scale=0.5),
                            waits=[("sT", last_f2_mm)], inc="sS")
                    f2evac = sch.cnt["sS"]

                for s in range(4):
                    for dhh in range(2):
                        tiles = [
                            (lambda e, col, s=s, db=4 * dhh + q: e.transpose(
                                psT[:, col: col + 128],
                                updf[:, 512 * db + 128 * s: 512 * db + 128 * (s + 1)],
                                ident[:]))
                            for q in range(4)]
                        psT_group(tiles,
                                  lambda e, col, s=s, dhh=dhh: e.tensor_tensor(
                                      xtile[:, 1024 * s + 512 * dhh:
                                            1024 * s + 512 * (dhh + 1)],
                                      xtile[:, 1024 * s + 512 * dhh:
                                            1024 * s + 512 * (dhh + 1)],
                                      psT[:, col: col + 512], ALU.add),
                                  waits_extra=[("sS", f2evac)])
                updf_guard = sch.cnt["sT"]
                x_new = sch.cnt["sV"]

                if outdst is not None:
                    sch.emit("sync", lambda e, Ti=Ti: e.dma_start(
                        x4(outdst)[Ti],
                        xtile[:].rearrange("p (s d) -> p s d", s=4)),
                        waits=[("sV", x_new), ("sIO", sch.cnt["sIO"])],
                        inc="sIO", inc_amt=16)
                    x_cons_s, x_cons_v = 0, x_new

                if ln2dst is not None:
                    for s in range(4):
                        sub = xtile[:, 1024 * s: 1024 * (s + 1)]
                        sch.emit("vector", lambda e, sub=sub, s=s: e.reduce_sum(
                            stat[:, s:s + 1], sub, AX),
                            waits=[("sV", x_new)], inc="sV")
                        sch.emit("scalar", lambda e, sub=sub, s=s: e.activation(
                            sqscr[:, 0:1024], sub, AF.Square,
                            accum_out=stat[:, 4 + s:5 + s]),
                            waits=[("sV", x_new), ("sS", sch.cnt["sS"])], inc="sS")
                    sq2 = sch.cnt["sS"]
                    sch.emit("vector", lambda e: e.tensor_scalar_mul(
                        stat[:, 8:12], stat[:, 0:4], 1.0 / 1024.0),
                    waits=[("sV", sch.cnt["sV"])], inc="sV")
                    sch.emit("vector", lambda e: e.tensor_scalar_mul(
                        stat[:, 12:16], stat[:, 4:8], 1.0 / 1024.0),
                        waits=[("sS", sq2), ("sV", sch.cnt["sV"])], inc="sV")
                    sch.emit("vector", lambda e: e.scalar_tensor_tensor(
                        stat[:, 16:20], stat[:, 8:12], -1.0, stat[:, 8:12],
                        ALU.mult, ALU.mult),
                        waits=[("sV", sch.cnt["sV"])], inc="sV")
                    sch.emit("vector", lambda e: e.tensor_add(
                        stat[:, 12:16], stat[:, 12:16], stat[:, 16:20]),
                        waits=[("sV", sch.cnt["sV"])], inc="sV")
                    sch.emit("vector", lambda e: e.tensor_scalar_add(
                        stat[:, 12:16], stat[:, 12:16], 1e-5),
                        waits=[("sV", sch.cnt["sV"])], inc="sV")
                    v2 = sch.cnt["sV"]
                    sch.emit("scalar", lambda e: e.activation(
                        stat[:, 16:20], stat[:, 12:16], AF.Ln),
                        waits=[("sV", v2), ("sS", sch.cnt["sS"])], inc="sS")
                    sch.emit("scalar", lambda e: e.activation(
                        stat[:, 20:24], stat[:, 16:20], AF.Exp, scale=-0.5),
                        waits=[("sS", sch.cnt["sS"])], inc="sS")
                    i2d = sch.cnt["sS"]
                    sch.emit("vector", lambda e: e.scalar_tensor_tensor(
                        stat[:, 24:28], stat[:, 8:12], -1.0, stat[:, 20:24],
                        ALU.mult, ALU.mult),
                        waits=[("sS", i2d), ("sV", sch.cnt["sV"])], inc="sV")
                    nmi2 = sch.cnt["sV"]
                    for s in range(4):
                        sch.emit("scalar", lambda e, s=s: e.activation(
                            updf[:, 1024 * s: 1024 * (s + 1)],
                            xtile[:, 1024 * s: 1024 * (s + 1)], AF.Identity,
                            bias=stat[:, 24 + s: 25 + s],
                            scale=stat[:, 20 + s: 21 + s]),
                            waits=[("sV", nmi2), ("sT", updf_guard),
                                   ("sS", sch.cnt["sS"])], inc="sS")
                    ln2_app = sch.cnt["sS"]
                    for c in range(8):
                        tiles = [
                            (lambda e, col, c=c, s=s: e.transpose(
                                psT[:, col: col + 128],
                                updf[:, 1024 * s + 128 * c: 1024 * s + 128 * (c + 1)],
                                ident[:]))
                            for s in range(4)]
                        psT_group(tiles,
                                  lambda e, col, c=c, Ti=Ti: e.tensor_copy(
                                      ln2dst[:, 2048 * c + 512 * Ti:
                                             2048 * c + 512 * Ti + 512],
                                      psT[:, col: col + 512]),
                                  waits_extra=[("sS", ln2_app)])
                    updf_guard = sch.cnt["sT"]
                    ln2_v = sch.cnt["sV"]
                    sch.emit("sync", lambda e, Ti=Ti: e.dma_start(
                        x4(spill_dst)[Ti],
                        xtile[:].rearrange("p (s d) -> p s d", s=4)),
                        waits=[("sV", x_new), ("sIO", sch.cnt["sIO"])],
                        inc="sIO", inc_amt=16)
                    x_cons_s = ln2_app
                    x_cons_v = ln2_v

        rstk = contextlib.ExitStack()
        ln2_fm = rstk.enter_context(
            nc.sbuf_tensor("ln2_fm", [128, 16384], BF16, side="right"))

        def ffn_bufs(p, tag):
            f1_sb = p.enter_context(nc.sbuf_tensor("f1" + tag, [128, 32768], BF16))
            mh_sb = p.enter_context(nc.sbuf_tensor("mh" + tag, [128, 16384], BF16))
            xtile = p.enter_context(nc.sbuf_tensor("xt" + tag, [128, 4096], F32))
            lnfm = p.enter_context(nc.sbuf_tensor("lf" + tag, [128, 4096], BF16))
            updf = p.enter_context(nc.sbuf_tensor("up" + tag, [128, 4096], BF16))
            f2sl = [p.enter_context(nc.sbuf_tensor(f"f2{tag}{i}", [128, 2048], BF16))
                    for i in range(3)]
            ctrl_sb = p.enter_context(nc.sbuf_tensor("ct" + tag, [128, 1024], F32))
            ln32c = p.enter_context(nc.sbuf_tensor("l3" + tag, [128, 512], F32))
            fbias_sb = p.enter_context(nc.sbuf_tensor("fb" + tag, [128, 32], F32))
            tie_sb = p.enter_context(nc.sbuf_tensor("ti" + tag, [128, 512], F32))
            E_sb = p.enter_context(nc.sbuf_tensor("E" + tag, [128, 4096], BF16))
            logit_sb = p.enter_context(nc.sbuf_tensor("lg" + tag, [128, 512], F32))
            perm_sb = p.enter_context(nc.sbuf_tensor("pm" + tag, [128, 512], BF16))
            hstage = p.enter_context(nc.sbuf_tensor("hs" + tag, [128, 1024], BF16))
            return (f1_sb, mh_sb, xtile, lnfm, updf, f2sl, ctrl_sb, fbias_sb,
                    tie_sb, E_sb, logit_sb, perm_sb, hstage, ln32c)

        p1 = contextlib.ExitStack()
        with p1:
            bufs1 = ffn_bufs(p1, "a")
            ffn_phase(x_in, xs1, ln2_fm, None, bufs1, {},
                      ntiles=p1_tiles, cut=cut)
            m1 = sch.marks()
            if stop_after == "p1":
                fin = 0
                if cut == "full":
                    for Ti in range(p1_tiles):
                        dma("sync", bufs1[2][:].rearrange("p (s d) -> p s d", s=4),
                            x4(xs1)[Ti], waits=[("sIO", m1["sIO"]),
                                                ("sT", m1["sT"]), ("sV", m1["sV"]),
                                                ("sS", m1["sS"])])
                        sch.emit("sync", lambda e, Ti=Ti: e.dma_start(
                            x4(out)[Ti],
                            bufs1[2][:].rearrange("p (s d) -> p s d", s=4)),
                            waits=[("sIO", sch.cnt["sIO"])], inc="sIO", inc_amt=16)
                        fin = sch.cnt["sIO"]
                else:
                    sch.emit("sync", lambda e: e.dma_start(
                        x4(out)[0],
                        bufs1[2][:].rearrange("p (s d) -> p s d", s=4)),
                        waits=[("sIO", m1["sIO"]), ("sT", m1["sT"]),
                               ("sV", m1["sV"]), ("sS", m1["sS"])],
                        inc="sIO", inc_amt=16)
                    fin = sch.cnt["sIO"]
                sch.emit("sync", None, waits=[("sIO", fin)])
                sch.flush(blk)
                return nc
            sch.flush(blk)

        sa = rstk.enter_context(
            nc.sbuf_tensor("sa", [128, 16384], BF16, side="right"))
        p2 = contextlib.ExitStack()
        with p2:
            Qb = p2.enter_context(nc.sbuf_tensor("Qb", [128, 16384], BF16))
            Kb = p2.enter_context(nc.sbuf_tensor("Kb", [128, 16384], BF16))
            Vfm = p2.enter_context(nc.sbuf_tensor("Vfm", [128, 2048], BF16))
            Vtm = p2.enter_context(nc.sbuf_tensor("Vtm", [128, 1024], BF16))
            Pst = [p2.enter_context(nc.sbuf_tensor(f"Pst{i}", [128, 2048], BF16))
                   for i in range(2)]
            PTs = p2.enter_context(nc.sbuf_tensor("PTs", [128, 2048], BF16))
            wsl = [p2.enter_context(nc.sbuf_tensor(f"wsl{i}", [128, 4096], BF16))
                   for i in range(5)]

            entry2 = [("sT", m1["sT"]), ("sV", m1["sV"]), ("sS", m1["sS"]),
                      ("sIO", m1["sIO"])]
            GROUPS = [(0, 512), (512, 512), (1024, 512), (1536, 512), (2048, 64)]
            SET = {0: [0, 1, 2, 3], 1: [3, 4, 5, 0]}
            bankS = {b: m1["sS"] for b in range(6)}
            bankV = {b: m1["sV"] for b in range(6)}
            g_consumed = {}
            exp_piece = {}
            o_evac = {}
            av_done = {}
            tr_done = {}

            for h in range(p2_heads):
                wdone = {}
                for g in range(5):
                    g0, gw = GROUPS[g]
                    w = ([("sT", g_consumed.get((h - 1, g), 0))]
                         if h > 0 else list(entry2))
                    src = wkqv[h].rearrange("(c p) e -> p c e", p=128)[:, :, g0:g0 + gw]
                    dst = wsl[g][:, 0:8 * gw].rearrange("p (c e) -> p c e", c=8)
                    sch.emit("gpsimd", lambda e, d=dst, s=src: e.dma_start(d, s),
                             waits=w + [("sW", sch.cnt["sW"])], inc="sW",
                             inc_amt=16)
                    wdone[g] = sch.cnt["sW"]

                ev = {}
                for g in range(5):
                    g0, gw = GROUPS[g]
                    for jj in range(max(1, gw // 128)):
                        j = (g0 // 128) + jj
                        ofw = min(128, gw - 128 * jj)
                        for ts in range(4):
                            w = [("sW", wdone[g]),
                                 ("sS", bankS[ts]), ("sV", bankV[ts])]
                            if h == 0 and j == 0 and ts == 0:
                                w += entry2
                            if j > 0:
                                w.append(ev[(j - 1, ts)])
                            outap = (bank(ts) if ofw == 128
                                     else psF[0:64, 512 * ts: 512 * ts + 512])
                            for c in range(8):
                                sch.emit("tensor",
                                         lambda e, g=g, gw=gw, c=c, jj=jj, ofw=ofw,
                                         outap=outap, ts=ts: e.matmul(
                                             outap,
                                             wsl[g][:, gw * c + 128 * jj:
                                                    gw * c + 128 * jj + ofw],
                                             ln2_fm[:, 2048 * c + 512 * ts:
                                                    2048 * c + 512 * (ts + 1)],
                                             start=(c == 0), stop=(c == 7)),
                                         waits=w if c == 0 else [],
                                         inc="sT" if c == 7 else None)
                            mmj = sch.cnt["sT"]
                            if j <= 7:
                                sch.emit("vector",
                                         lambda e, j=j, ts=ts, h=h: e.tensor_scalar_add(
                                             Kb[:, 2048 * j + 512 * ts:
                                                2048 * j + 512 * (ts + 1)],
                                             bank(ts),
                                             bkqv_sb[:, 17 * h + j: 17 * h + j + 1]),
                                         waits=[("sT", mmj), ("sV", v_init)],
                                         inc="sV")
                                ev[(j, ts)] = ("sV", sch.cnt["sV"])
                                bankV[ts] = sch.cnt["sV"]
                            elif j <= 15:
                                sch.emit("scalar",
                                         lambda e, j=j, ts=ts, h=h: e.activation(
                                             Qb[:, 2048 * (j - 8) + 512 * ts:
                                                2048 * (j - 8) + 512 * (ts + 1)],
                                             bank(ts), AF.Identity,
                                             bias=bkqvq_sb[:, 17 * h + j: 17 * h + j + 1],
                                             scale=1.0 / 32.0),
                                         waits=[("sT", mmj), ("sV", v_init)],
                                         inc="sS")
                                ev[(j, ts)] = ("sS", sch.cnt["sS"])
                                bankS[ts] = sch.cnt["sS"]
                            else:
                                sch.emit("vector",
                                         lambda e, ts=ts, h=h: e.tensor_scalar_add(
                                             Vfm[0:64, 512 * ts: 512 * (ts + 1)],
                                             psF[0:64, 512 * ts: 512 * ts + 512],
                                             bkqv_sb[0:64, 17 * h + 16: 17 * h + 17]),
                                         waits=[("sT", mmj), ("sV", v_init)],
                                         inc="sV")
                                ev[(j, ts)] = ("sV", sch.cnt["sV"])
                                bankV[ts] = sch.cnt["sV"]
                        if jj == max(1, gw // 128) - 1:
                            g_consumed[(h, g)] = sch.cnt["sT"]
                kq_s = max([v for (s_, v) in ev.values() if s_ == "sS"], default=0)
                kq_v = max([v for (s_, v) in ev.values() if s_ == "sV"], default=0)
                vfm_done = max(ev[(16, ts)][1] for ts in range(4))

                for vg in range(4):
                    tiles = [
                        (lambda e, col, i2=4 * vg + q: e.transpose(
                            psT[:, col: col + 128],
                            Vfm[:, 128 * i2: 128 * (i2 + 1)], ident[:]))
                        for q in range(4)]
                    psT_group(tiles,
                              lambda e, col, vg=vg: e.tensor_copy(
                                  Vtm[:, 256 * vg: 256 * (vg + 1)].rearrange(
                                      "p (s v) -> p s v", s=4),
                                  psT[:, col: col + 512].rearrange(
                                      "p (s v) -> p s v", s=4)[:, :, 0:64]),
                              waits_extra=[("sV", vfm_done),
                                           ("sT", av_done.get((h - 1, 15), 0))])
                vtm_done = sch.cnt["sV"]

                for qb in range(16):
                    par = qb % 2
                    st = SET[par]
                    W = 128 * (qb + 1)
                    nb = (W + 511) // 512
                    for c in range(8):
                        for i in range(nb):
                            kw = min(512, W - 512 * i)
                            w = []
                            if c == 0:
                                w = [("sS", bankS[st[i]]), ("sV", bankV[st[i]]),
                                     ("sS", kq_s), ("sV", kq_v)]
                                if i == 3:
                                    w = [("sS", bankS[st[i]]), ("sV", bankV[st[i]])]
                            sch.emit("tensor",
                                     lambda e, c=c, i=i, kw=kw, st=st, qb=qb: e.matmul(
                                         bank(st[i], kw),
                                         Qb[:, 2048 * c + 128 * qb:
                                            2048 * c + 128 * (qb + 1)],
                                         Kb[:, 2048 * c + 512 * i:
                                            2048 * c + 512 * i + kw],
                                         start=(c == 0), stop=(c == 7)),
                                     waits=w,
                                     inc="sT" if (c == 7 and i == nb - 1) else None)
                    sc_done = sch.cnt["sT"]
                    dbk = st[nb - 1]
                    doff = (W - 512 * (nb - 1)) - 128
                    sch.emit("vector", lambda e, dbk=dbk, doff=doff: e.tensor_tensor(
                        bank(dbk, 128, doff), bank(dbk, 128, doff), mneg[:],
                        ALU.add),
                        waits=[("sT", sc_done), ("sV", sch.cnt["sV"])], inc="sV")
                    for i in range(nb):
                        kw = min(512, W - 512 * i)
                        sch.emit("vector", lambda e, i=i, kw=kw, st=st: e.reduce_max(
                            stat[:, 20 + i: 21 + i], bank(st[i], kw), AX),
                            waits=[("sV", sch.cnt["sV"])] if i == 0 else [],
                            inc="sV" if i == nb - 1 else None)
                    sch.emit("vector", lambda e, nb=nb, par=par: e.reduce_max(
                        stat[:, 26 + par: 27 + par], stat[:, 20: 20 + nb], AX,
                        negate=True),
                        waits=[("sV", sch.cnt["sV"])], inc="sV")
                    mxd = sch.cnt["sV"]
                    order = [3, 0, 1, 2] if nb == 4 else list(range(nb))
                    for i in order:
                        kw = min(512, W - 512 * i)
                        sch.emit("scalar",
                                 lambda e, i=i, kw=kw, st=st, par=par: e.activation(
                                     Pst[par][:, 512 * i: 512 * i + kw],
                                     bank(st[i], kw), AF.Exp,
                                     bias=stat[:, 26 + par: 27 + par],
                                     accum_out=smx[:, 4 * par + i: 4 * par + i + 1]),
                                 waits=[("sV", mxd),
                                        ("sT", tr_done.get((h, qb - 2), 0))],
                                 inc="sS")
                        exp_piece[(h, qb, i)] = sch.cnt["sS"]
                        bankS[st[i]] = sch.cnt["sS"]
                    expall = sch.cnt["sS"]
                    sch.emit("vector", lambda e, nb=nb, par=par: e.reduce_sum(
                        smx[:, 8 + 2 * par: 9 + 2 * par],
                        smx[:, 4 * par: 4 * par + nb], AX),
                        waits=[("sS", expall)], inc="sV")
                    ls_mark = sch.cnt["sV"]
                    sch.emit("vector", lambda e, par=par: e.reciprocal(
                        smx[:, 12 + 2 * par: 14 + 2 * par],
                        smx[:, 8 + 2 * par: 10 + 2 * par]),
                        waits=[("sV", ls_mark)], inc="sV")
                    rc_mark = sch.cnt["sV"]
                    sch.emit("vector", lambda e, par=par, W=W: e.tensor_scalar_mul(
                        Pst[par][:, 0:W], Pst[par][:, 0:W],
                        smx[:, 12 + 2 * par: 13 + 2 * par]),
                        waits=[("sV", rc_mark)], inc="sV")
                    nrm = sch.cnt["sV"]
                    ntile = qb + 1
                    for tg in range((ntile + 3) // 4):
                        cnt4 = min(4, ntile - 4 * tg)
                        tiles = [
                            (lambda e, col, i2=4 * tg + q, par=par: e.transpose(
                                psT[:, col: col + 128],
                                Pst[par][:, 128 * i2: 128 * (i2 + 1)], ident[:]))
                            for q in range(cnt4)]
                        psT_group(tiles,
                                  lambda e, col, tg=tg, cnt4=cnt4: e.tensor_copy(
                                      PTs[:, 512 * tg: 512 * tg + 128 * cnt4],
                                      psT[:, col: col + 128 * cnt4]),
                                  waits_extra=[("sV", nrm),
                                               ("sT", av_done.get(
                                                   (h, qb - 1),
                                                   av_done.get((h - 1, 15), 0)))])
                    tr_done[(h, qb)] = sch.cnt["sT"]
                    ptv = sch.cnt["sV"]
                    ob = st[1]
                    for i2 in range(ntile):
                        w = []
                        if i2 == 0:
                            w = [("sV", max(ptv, vtm_done)),
                                 ("sS", bankS[ob]), ("sV", bankV[ob])]
                        sch.emit("tensor",
                                 lambda e, i2=i2, ob=ob, ntile=ntile: e.matmul(
                                     psF[0:64, 512 * ob: 512 * ob + 128],
                                     Vtm[:, 64 * i2: 64 * (i2 + 1)],
                                     PTs[:, 128 * i2: 128 * (i2 + 1)],
                                     start=(i2 == 0), stop=(i2 == ntile - 1)),
                                 waits=w, inc="sT" if i2 == ntile - 1 else None)
                    av_done[(h, qb)] = sch.cnt["sT"]
                    hp, ro = h // 2, 64 * (h % 2)
                    sch.emit("scalar", lambda e, hp=hp, ro=ro, qb=qb, ob=ob: e.activation(
                        sa[ro:ro + 64, 2048 * hp + 128 * qb:
                           2048 * hp + 128 * (qb + 1)],
                        psF[0:64, 512 * ob: 512 * ob + 128], AF.Copy),
                        waits=[("sT", av_done[(h, qb)])], inc="sS")
                    o_evac[(h, qb)] = sch.cnt["sS"]
                    bankS[ob] = sch.cnt["sS"]
            if stop_after == "p2":
                ov = out.rearrange("(a p) d -> p a d", p=128)
                wt = [("sS", sch.cnt["sS"]), ("sT", sch.cnt["sT"]),
                      ("sV", sch.cnt["sV"]), ("sW", sch.cnt["sW"])]
                if p2_heads == H:
                    sch.emit("gpsimd", lambda e: e.dma_start(
                        ov, sa[:].rearrange("p (a d) -> p a d", a=16)),
                        waits=wt, inc="sW", inc_amt=16)
                else:
                    ovf = out.rearrange("(a p) d -> a p d", p=128)
                    dbg = [(Pst[0][:], 0), (Pst[1][:], 2), (PTs[:], 4),
                           (Vtm[:], 6), (sa[:, 0:2048], 8),
                           (smx[:], 10), (stat[:], 11)]
                    for t_, ai in dbg:
                        n_ = t_.shape[-1]
                        for piece in range((n_ + 1023) // 1024):
                            pw = min(1024, n_ - 1024 * piece)
                            sch.emit("gpsimd", lambda e, t_=t_, ai=ai, piece=piece, pw=pw: e.dma_start(
                                ovf[ai + piece][:, 0:pw],
                                t_[:, 1024 * piece: 1024 * piece + pw]),
                                waits=wt + [("sW", sch.cnt["sW"])],
                                inc="sW", inc_amt=16)
                sch.emit("gpsimd", None, waits=[("sW", sch.cnt["sW"])])
                sch.flush(blk)
                return nc
            sch.flush(blk)
            m2 = sch.marks()

        p3 = contextlib.ExitStack()
        with p3:
            wp_sb = p3.enter_context(nc.sbuf_tensor("wp_sb", [128, 8192], BF16))
            x1t = p3.enter_context(nc.sbuf_tensor("x1t", [128, 4096], F32))
            updf3 = p3.enter_context(nc.sbuf_tensor("updf3", [128, 4096], BF16))

            entry3 = [("sT", m2["sT"]), ("sV", m2["sV"]), ("sS", m2["sS"]),
                      ("sIO", m2["sIO"])]
            dma("gpsimd", wp_sb[:].rearrange("p (c d) -> p c d", c=8),
                wp.rearrange("(c p) d -> p c d", p=128), waits=entry3)
            wp_done = sch.cnt["sW"]

            ev3 = {}
            x1_used_v = 0
            for Ti in range(NT):
                dma("sync", x1t[:].rearrange("p (s d) -> p s d", s=4),
                    x4(xs1)[Ti], waits=entry3 + [("sV", x1_used_v)])
                x1_here = sch.cnt["sIO"]
                for dq in range(2):
                    for dbl in range(4):
                        db = 4 * dq + dbl
                        for c in range(8):
                            w = []
                            if c == 0:
                                w = [("sW", wp_done),
                                     ("sS", bankS[dbl]), ("sV", bankV[dbl])]
                                pv = (ev3.get((Ti, db - 4))
                                      or ev3.get((Ti - 1, db + 4)))
                                if pv:
                                    w.append(pv)
                            sch.emit("tensor",
                                     lambda e, c=c, db=db, dbl=dbl, Ti=Ti: e.matmul(
                                         bank(dbl),
                                         wp_sb[:, 1024 * c + 128 * db:
                                               1024 * c + 128 * (db + 1)],
                                         sa[:, 2048 * c + 512 * Ti:
                                            2048 * c + 512 * (Ti + 1)],
                                         start=(c == 0), stop=(c == 7)),
                                     waits=w, inc="sT" if c == 7 else None)
                        mm3 = sch.cnt["sT"]
                        sch.emit("scalar", lambda e, db=db, dbl=dbl: e.activation(
                            updf3[:, 512 * db: 512 * (db + 1)], bank(dbl),
                            AF.Identity, bias=bp_sb[:, db: db + 1]),
                            waits=[("sT", mm3), ("sIO", c_loaded)], inc="sS")
                        ev3[(Ti, db)] = ("sS", sch.cnt["sS"])
                        bankS[dbl] = sch.cnt["sS"]
                upd3 = max(ev3[(Ti, db)][1] for db in range(8))
                for s in range(4):
                    for dhh in range(2):
                        tiles = [
                            (lambda e, col, s=s, db=4 * dhh + q: e.transpose(
                                psT[:, col: col + 128],
                                updf3[:, 512 * db + 128 * s:
                                      512 * db + 128 * (s + 1)],
                                ident[:]))
                            for q in range(4)]
                        psT_group(tiles,
                                  lambda e, col, s=s, dhh=dhh: e.tensor_tensor(
                                      x1t[:, 1024 * s + 512 * dhh:
                                          1024 * s + 512 * (dhh + 1)],
                                      x1t[:, 1024 * s + 512 * dhh:
                                          1024 * s + 512 * (dhh + 1)],
                                      psT[:, col: col + 512], ALU.add),
                                  waits_extra=[("sS", upd3), ("sIO", x1_here)])
                x2done = sch.cnt["sV"]
                sch.emit("sync", lambda e, Ti=Ti: e.dma_start(
                    x4(xs2)[Ti], x1t[:].rearrange("p (s d) -> p s d", s=4)),
                    waits=[("sV", x2done), ("sIO", sch.cnt["sIO"])],
                    inc="sIO", inc_amt=16)
                x1_used_v = x2done
            if stop_after == "p3":
                fin = 0
                for Ti in range(NT):
                    dma("sync", x1t[:].rearrange("p (s d) -> p s d", s=4),
                        x4(xs2)[Ti], waits=[("sV", sch.cnt["sV"])])
                    sch.emit("sync", lambda e, Ti=Ti: e.dma_start(
                        x4(out)[Ti],
                        x1t[:].rearrange("p (s d) -> p s d", s=4)),
                        waits=[("sIO", sch.cnt["sIO"])], inc="sIO", inc_amt=16)
                    fin = sch.cnt["sIO"]
                sch.emit("sync", None, waits=[("sIO", fin)])
                sch.flush(blk)
                return nc
            sch.flush(blk)
            m3 = sch.marks()

        rstk.close()

        p4 = contextlib.ExitStack()
        with p4:
            bufs4 = ffn_bufs(p4, "b")
            ffn_phase(xs2, None, None, out, bufs4,
                      {"sT": m3["sT"], "sV": m3["sV"], "sS": m3["sS"],
                       "sIO": m3["sIO"]})
            fin = sch.cnt["sIO"]
            sch.emit("sync", None, waits=[("sIO", fin)])
            sch.flush(blk)

    return nc


def kernel(L0_attn_x, L0_attn_Wkqv, L0_attn_bkqv, L0_attn_Wp, L0_attn_bp,
           L0_ffn_controller, L0_ffn_f1, L0_ffn_bias, L0_ffn_f2,
           _stop_after=None, _p2_heads=H):
    x = np.ascontiguousarray(np.asarray(L0_attn_x, dtype=np.float32))
    wkqv_a = np.ascontiguousarray(np.asarray(L0_attn_Wkqv, dtype=np.float32))
    bkqv = np.asarray(L0_attn_bkqv, dtype=np.float32)
    wp_a = np.ascontiguousarray(np.asarray(L0_attn_Wp, dtype=np.float32))
    bp = np.asarray(L0_attn_bp, dtype=np.float32)
    ctrl_a = np.ascontiguousarray(
        np.asarray(L0_ffn_controller, dtype=np.float32).reshape(D, 128))
    f1_a = np.ascontiguousarray(
        np.asarray(L0_ffn_f1, dtype=np.float32).reshape(D, NESF))
    fb = np.asarray(L0_ffn_bias, dtype=np.float32).reshape(NESF)
    f2_a = np.ascontiguousarray(
        np.asarray(L0_ffn_f2, dtype=np.float32).reshape(NESF, D))

    bkqv_c = np.zeros((128, H * 17), dtype=np.float32)
    for h in range(H):
        for j in range(16):
            bkqv_c[:, 17 * h + j] = bkqv[h, 128 * j: 128 * (j + 1)]
        bkqv_c[0:64, 17 * h + 16] = bkqv[h, 2048:2112]
    bp_c = np.ascontiguousarray(bp.reshape(8, 128).T)
    fbias_c = np.ascontiguousarray(fb.reshape(32, 128).T)

    ident, mneg, tie_tile, E = _host_consts()

    nc = build_program(stop_after=_stop_after, p2_heads=_p2_heads)
    shared = {
        "wkqv": wkqv_a, "bkqv_c": bkqv_c, "wp": wp_a, "bp_c": bp_c,
        "ctrl": ctrl_a, "f1": f1_a, "fbias_c": fbias_c, "f2": f2_a,
        "c_ident": ident, "c_mneg": mneg, "c_tie": tie_tile, "c_E": E,
    }
    in_maps = [dict(shared, x=np.ascontiguousarray(x[i])) for i in range(N_CORE)]
    res = run_bass_kernel_spmd(nc, in_maps, core_ids=list(range(N_CORE)))
    return np.stack([np.asarray(r["out"], dtype=np.float32) for r in res.results])
